# version 09
# speedup vs baseline: 1.3940x; 1.0038x over previous
import numpy as np
import concourse.bass as bass
import concourse.mybir as mybir

F32 = mybir.dt.float32
BF16 = mybir.dt.bfloat16
AF = mybir.ActivationFunctionType
ALU = mybir.AluOpType
AX = mybir.AxisListType

ENGS = ("sync", "tensor", "vector", "scalar", "gpsimd")


FORCE_DRAIN = set()


class Prog:
    def __init__(self, nc):
        self.nc = nc
        self.steps = []

    def _add(self, eng, kind, fn, deps):
        for d in deps:
            assert 0 <= d < len(self.steps), f"bad dep {d} at step {len(self.steps)}"
        self.steps.append(dict(eng=eng, kind=kind, fn=fn, deps=list(deps)))
        return len(self.steps) - 1

    def dma(self, fn, deps=(), n=1):
        i = self._add("sync", "dma", fn, deps)
        self.steps[i]["n"] = n
        return i

    def gdma(self, fn, deps=(), n=1):
        i = self._add("gpsimd", "dma", fn, deps)
        self.steps[i]["n"] = n
        return i

    def t(self, fn, deps=()):
        return self._add("tensor", "mm", fn, deps)

    def v(self, fn, deps=(), tag=None, n=1):
        i = self._add("vector", "op", fn, deps)
        self.steps[i]["tag"] = tag
        self.steps[i]["n"] = n
        return i

    def a(self, fn, deps=(), n=1):
        i = self._add("scalar", "op", fn, deps)
        self.steps[i]["n"] = n
        return i

    def g(self, fn, deps=(), n=1):
        i = self._add("gpsimd", "op", fn, deps)
        self.steps[i]["n"] = n
        return i

    def barrier(self):
        last = {}
        for i, s in enumerate(self.steps):
            last[s["eng"]] = i
        deps = sorted(last.values())
        ids = []
        for eng in ENGS:
            fn = (lambda e: None)
            kind = "op"
            ids.append(self._add(eng, kind, fn, deps))
        return ids

    def finalize(self, block, es):
        nc = self.nc
        needed = set(i for i, s in enumerate(self.steps) if s["kind"] == "dma")
        for s in self.steps:
            for d in s["deps"]:
                if self.steps[d]["eng"] != s["eng"] or self.steps[d]["kind"] == "dma":
                    needed.add(d)
        sems = {eng: es.enter_context(nc.semaphore(f"sem_{eng}")) for eng in ENGS}
        incval = {}
        counters = {eng: 0 for eng in ENGS}
        for i, s in enumerate(self.steps):
            if i in needed:
                amt = 16 * s.get("n", 1) if s["kind"] == "dma" else s.get("n", 1)
                counters[s["eng"]] += amt
                incval[i] = counters[s["eng"]]

        def make_stream(eng):
            mysteps = [(i, s) for i, s in enumerate(self.steps) if s["eng"] == eng]

            def stream(e):
                for i, s in mysteps:
                    waits = {}
                    for d in s["deps"]:
                        ds = self.steps[d]
                        if ds["eng"] == eng and ds["kind"] != "dma":
                            continue
                        assert d in incval, f"dep {d} not marked"
                        w = waits.get(ds["eng"], 0)
                        waits[ds["eng"]] = max(w, incval[d])
                    for weng, val in waits.items():
                        e.wait_ge(sems[weng], val)
                    ret = s["fn"](e)
                    if i in needed:
                        if s["kind"] == "dma":
                            rl = ret if isinstance(ret, (list, tuple)) else [ret]
                            assert len(rl) == s.get("n", 1), f"dma step {i}: {len(rl)} != n"
                            for r in rl:
                                r.then_inc(sems[eng], 16)
                        elif s["kind"] == "mm":
                            assert ret is not None, f"mm step {i} must return last instr"
                            ret.then_inc(sems[eng], 1)
                        elif ret is not None and eng not in FORCE_DRAIN and s.get("tag") not in FORCE_DRAIN:
                            rl = ret if isinstance(ret, (list, tuple)) else [ret]
                            assert len(rl) == s.get("n", 1), f"op step {i}: {len(rl)} != n"
                            for r in rl:
                                r.then_inc(sems[eng], 1)
                        else:
                            assert s.get("n", 1) == 1, f"drained op step {i} must have n=1"
                            e.drain().then_inc(sems[eng], 1)
            return stream

        for eng in ENGS:
            getattr(block, eng)(make_stream(eng))


D, H, DM, C, K, SJ, TT = 128, 2, 256, 1024, 16, 8, 128
EPS5, EPS6 = 1e-5, 1e-6


def build(n_sc, max_steps=None, dbg=False):
    L = TT * n_sc
    nc = bass.Bass("TRN2", target_bir_lowering=False)
    T = nc.dram_tensor
    inp = {}

    def I(name, shape, dt=F32):
        ap = T(name, list(shape), dt, kind="ExternalInput").ap()
        inp[name] = ap
        return ap

    x_seq = I("x_seq", [L, C])
    down_w = I("down_w", [C, DM]); down_b = I("down_b", [1, DM])
    up_w = I("up_w", [DM, C]); up_b = I("up_b", [1, C])
    wq = I("wq", [DM, DM]); wk = I("wk", [DM, DM]); wv = I("wv", [DM, DM]); wo = I("wo", [DM, DM])
    W1in = I("W1in", [H, D, D]); b1in = I("b1in", [1, H * D])
    ttt_g = I("ttt_g", [1, DM]); ttt_b = I("ttt_b", [1, DM])
    lrwT = I("lrwT", [DM, H]); lr_b = I("lr_b", [1, H])
    tok_idx = I("tok_idx", [K, 1]); recip16 = I("recip16", [K, 1])
    post_w = I("post_w", [1, DM]); post_b = I("post_b", [1, DM])
    ln_w = I("ln_w", [1, DM]); ln_b = I("ln_b", [1, DM])
    postw_c = I("postw_c", [128, 2]); postb_c = I("postb_c", [128, 2])
    lnw_c = I("lnw_c", [128, 2]); lnb_c = I("lnb_c", [128, 2])
    WrT = I("WrT", [C, C]); WkT = I("WkT", [C, C]); WvT = I("WvT", [C, C])
    mur_c = I("mur_c", [128, 8]); muk_c = I("muk_c", [128, 8])
    identb = I("identb", [128, 128], BF16)
    maskU = I("maskU", [128, 128]); maskEQT = I("maskEQT", [128, 128])

    y = T("y", [L, C], F32, kind="ExternalOutput").ap()

    scr = {}
    def S(name, shape, dt=F32):
        scr[name] = T(name, list(shape), dt, kind=("ExternalOutput" if dbg else "Internal")).ap()
        return scr[name]

    scrXKT = S("scrXKT", [n_sc, 128, 256], BF16)
    scrXQT = S("scrXQT", [n_sc, 128, 256], BF16)
    scrLEXK = S("scrLEXK", [n_sc, 128, 256], BF16)
    scrGT = S("scrGT", [n_sc, 128, 256], BF16)
    scrMK = S("scrMK", [n_sc, H, 128, 128], BF16)
    scrMQ = S("scrMQ", [n_sc, H, 128, 128], BF16)
    scrLE = S("scrLE", [n_sc, 128, 2], BF16)
    scrNMK = S("scrNMK", [n_sc, 1, 256])
    scrNMQ = S("scrNMQ", [n_sc, 1, 256])
    scrXQGB = S("scrXQGB", [n_sc, 128, 256], BF16)
    scrOUTS = S("scrOUTS", [n_sc, 16, 2048])
    if dbg:
        scrDBG = S("scrDBG", [2 * n_sc, 4, 16, 256])
        scrDBGQ = S("scrDBGQ", [2 * n_sc, 4, 16, 256])
    scrX2 = S("scrX2", [n_sc, 128, C])

    from contextlib import ExitStack
    es = ExitStack()
    with es:
        def sb(name, shape, dt=F32):
            return es.enter_context(nc.sbuf_tensor(name, list(shape), dt))
        def psum(name):
            return es.enter_context(nc.psum_tensor(name, [128, 512], F32))

        stg = sb("stg", [128, 4096])
        dwbf = sb("dwbf", [128, 8, 256], BF16)
        upbf = sb("upbf", [128, 2, 1024], BF16)
        wqbf = sb("wqbf", [128, 2, 256], BF16)
        wkbf = sb("wkbf", [128, 2, 256], BF16)
        wvnbf = sb("wvnbf", [128, 2, 256], BF16)
        wobf = sb("wobf", [128, 2, 256], BF16)
        wrtb = sb("wrtb", [128, 8, 1024], BF16)
        wktb = sb("wktb", [128, 8, 1024], BF16)
        wvtb = sb("wvtb", [128, 8, 1024], BF16)
        lrwbf = sb("lrwbf", [128, 2, 2], BF16)
        sW1 = sb("sW1", [128, 2, 128])
        W1f = sb("W1f", [128, 2, 256])
        W1b = sb("W1b", [128, 2, 256], BF16)
        b1f = sb("b1f", [1, 2, 256])
        grow = sb("grow", [1, DM]); brow = sb("brow", [1, DM]); g2row = sb("g2row", [1, DM])
        dbrow = sb("dbrow", [1, DM]); upbrow = sb("upbrow", [1, C])
        pwrow = sb("pwrow", [1, DM]); pbrow = sb("pbrow", [1, DM])
        lwrow = sb("lwrow", [1, DM]); lbrow = sb("lbrow", [1, DM])
        pwc = sb("pwc", [128, 2]); pbc = sb("pbc", [128, 2])
        lwc = sb("lwc", [128, 2]); lbc = sb("lbc", [128, 2])
        lrbrow = sb("lrbrow", [1, 2])
        tokc = sb("tokc", [16, 1]); r16 = sb("r16", [16, 1])
        murc = sb("murc", [128, 8]); mukc = sb("mukc", [128, 8])
        sid = sb("sid", [128, 128], BF16)
        smU = sb("smU", [128, 128]); smEQ = sb("smEQ", [128, 128])
        bwo = sb("bwo", [1, DM]); bup2 = sb("bup2", [1, C])
        GREP = sb("GREP", [128, DM]); G2REP = sb("G2REP", [128, DM])
        SCALEQ = sb("SCALEQ", [128, 128]); SCALEK = sb("SCALEK", [128, 128])
        tid = sb("tid", [16, 1]); tidrow = sb("tidrow", [1, 16]); tidrow128 = sb("tidrow128", [1, 128])
        t15r = sb("t15r", [128, 1]); nT15d = sb("nT15d", [128, 1]); t15a = sb("t15a", [1, 1])
        lrbB = sb("lrbB", [128, 2])
        w1sum = sb("w1sum", [128, 2]); w1sumb = sb("w1sumb", [128, 2], BF16)
        sb1c = sb("sb1c", [1, 2])
        onesr = sb("onesr", [1, 512])
        onesb = sb("onesb", [1, 512], BF16)
        b1b2 = sb("b1b2", [1, 2, 512], BF16)
        zrowb = sb("zrowb", [1, 1024], BF16)
        c_eps5 = sb("c_eps5", [128, 1]); c_eps6 = sb("c_eps6", [128, 1])
        c_zero = sb("c_zero", [128, 1])

        xbuf = sb("xbuf", [128, 2, C])
        bnb = sb("bnb", [128, 16]); mv1 = sb("mv1", [128, 8]); lnt = sb("lnt", [128, 8])
        xln = sb("xln", [128, 2, C], BF16)
        xlnT = sb("xlnT", [128, 8, 128], BF16)
        hT = sb("hT", [128, 2, 128], BF16)
        ttlr = sb("ttlr", [128, 2])
        nglec = sb("nglec", [128, 2]); nglecb = sb("nglecb", [128, 2], BF16)
        nglr = sb("nglr", [128, 2])
        xkTs = sb("xkTs", [128, 256], BF16); xqTs = sb("xqTs", [128, 256], BF16)
        lexk = sb("lexk", [128, 256], BF16); gTt = sb("gTt", [128, 256], BF16)
        xqgb = sb("xqgb", [128, 256], BF16)
        mtmp = sb("mtmp", [128, 128]); svf = sb("svf", [16, 1]); qvf = sb("qvf", [16, 1])
        MKt = sb("MKt", [128, 2, 128], BF16); MQt = sb("MQt", [128, 2, 128], BF16)
        nmK = sb("nmK", [1, 256]); nmQ = sb("nmQ", [1, 256])

        wXKT = sb("wXKT", [128, 2, 256], BF16); wXQT = sb("wXQT", [128, 2, 256], BF16)
        wLEXK = sb("wLEXK", [16, 2, 2048], BF16); wGT = sb("wGT", [16, 2, 2048], BF16)
        wMK = sb("wMK", [16, 2, 1024], BF16); wMQ = sb("wMQ", [16, 2, 1024], BF16)
        wLE = sb("wLE", [16, 2, 16], BF16)
        wNMK = sb("wNMK", [1, 2, 256]); wNMQ = sb("wNMQ", [1, 2, 256])
        wXQGB = sb("wXQGB", [16, 2, 2048], BF16)
        sxcsq = sb("sxcsq", [16, 512])
        sxc = sxcsq[:, 0:256]
        sms = sb("sms", [16, 4]); qms = sb("qms", [16, 4])
        svar = sb("svar", [16, 2]); smsq = sb("smsq", [16, 2]); sstd = sb("sstd", [16, 2])
        sinv = sb("sinv", [16, 2]); sninv = sb("sninv", [16, 2])
        sxh = sb("sxh", [16, 256]); sgyp = sb("sgyp", [16, 256])
        sgp = sb("sgp", [16, 512])
        ss12 = sb("ss12", [16, 4]); sm12 = sb("sm12", [16, 4])
        ssu = sb("ssu", [16, 256])
        gradb = sb("gradb", [16, 256], BF16)
        qxcsq = sb("qxcsq", [16, 512])
        qxc = qxcsq[:, 0:256]; qsq = qxcsq[:, 256:512]
        qvar = sb("qvar", [16, 2]); qmsq = sb("qmsq", [16, 2]); qstd = sb("qstd", [16, 2]); qinv = sb("qinv", [16, 2])
        qt1 = sb("qt1", [16, 256]); qt2 = sb("qt2", [16, 256])
        outsb = sb("outsb", [16, 2, 256])

        otile = sb("otile", [128, 256])
        bnb3 = sb("bnb3", [128, 16]); mv3 = sb("mv3", [128, 8]); lnt3 = sb("lnt3", [128, 8])
        bnb4 = sb("bnb4", [128, 16]); mv4 = sb("mv4", [128, 8]); lnt4 = sb("lnt4", [128, 8])
        xhat1 = sb("xhat1", [128, 256], BF16); xhat2 = sb("xhat2", [128, 256], BF16)
        xh3 = stg[:, 1024:2048]
        x2t = stg[:, 1024:2048]
        mtmp2 = stg[:, 3072:3200]
        xln4 = sb("xln4", [128, C], BF16)
        xlnT4 = sb("xlnT4", [128, 8, 128], BF16)
        xprevT = sb("xprevT", [128, 8, 128], BF16)
        dTt = sb("dTt", [128, 1024], BF16)
        xrT = sb("xrT", [128, 1024], BF16); xkmT = sb("xkmT", [128, 8, 128], BF16)
        prevc = sb("prevc", [128, 8])
        rsg = dTt
        kact = xrT
        kactT = xprevT
        x3b = sb("x3b", [128, C])
        x24 = sb("x24", [128, C])
        xh4 = stg[:, 2048:3072]
        ffo = stg[:, 0:1024]
        fout = ffo
        xt1T = sb("xt1T", [128, 2, 128], BF16)

        pA, pB, pC, pD, pE, pF, pG, pH = (psum(n) for n in "pA pB pC pD pE pF pG pH".split())

        P = Prog(nc)

        def clr(e, bank, n=1024):
            return e.matmul(bank[:, 0:n // 2], zrowb[0:1, 0:128], zrowb[0:1, 0:n // 2], start=True, stop=False)

        ld = {}
        ld["dw"] = P.dma(lambda e: e.dma_start(out=stg[:, 0:2048].rearrange("p (ci n) -> p ci n", n=256), in_=down_w.rearrange("(ci p) n -> p ci n", p=128)))
        ld["up"] = P.dma(lambda e: e.dma_start(out=stg[:, 2048:4096].rearrange("p (ci n) -> p ci n", n=1024), in_=up_w.rearrange("(ci p) n -> p ci n", p=128)))
        ld["W1"] = P.dma(lambda e: e.dma_start(out=sW1[:], in_=W1in.rearrange("h p f -> p h f")))
        ld["b1"] = P.dma(lambda e: e.dma_start(out=b1f[:, 0, :], in_=b1in[:]))
        rows = [(grow, ttt_g), (brow, ttt_b), (dbrow, down_b), (upbrow, up_b),
                (pwrow, post_w), (pbrow, post_b), (lwrow, ln_w), (lbrow, ln_b), (lrbrow, lr_b)]
        for i, (t_, src) in enumerate(rows):
            ld[f"row{i}"] = P.dma(lambda e, t_=t_, src=src: e.dma_start(out=t_[:], in_=src[:]))
        cols = [(pwc, postw_c), (pbc, postb_c), (lwc, lnw_c), (lbc, lnb_c),
                (tokc, tok_idx), (r16, recip16), (murc, mur_c), (mukc, muk_c),
                (sid, identb), (smU, maskU), (smEQ, maskEQT)]
        for i, (t_, src) in enumerate(cols):
            ld[f"col{i}"] = P.dma(lambda e, t_=t_, src=src: e.dma_start(out=t_[:], in_=src[:]))
        allld = list(ld.values())

        def vmemset(e):
            e.memset(onesr[:], 1.0)
            e.memset(onesb[:], 1.0)
            e.memset(zrowb[:], 0.0)
            e.memset(c_eps5[:], EPS5)
            e.memset(c_eps6[:], EPS6)
            e.memset(c_zero[:], 0.0)
        s_ms = P.v(vmemset)

        def vcast1(e):
            for ci in range(8):
                e.tensor_copy(dwbf[:, ci, :], stg[:, 256 * ci:256 * (ci + 1)])
        s_cast1 = P.v(vcast1, deps=allld)

        def t_fold2(e):
            clr(e, pF)
            for ci in range(2):
                e.matmul(pF[0:1, 0:512], lbc[:, ci:ci + 1], stg[:, 2048 + 1024 * ci:2048 + 1024 * ci + 512], start=False, stop=False)
            clr(e, pH)
            r = None
            for ci in range(2):
                r = e.matmul(pH[0:1, 0:512], lbc[:, ci:ci + 1], stg[:, 2048 + 1024 * ci + 512:2048 + 1024 * (ci + 1)], start=False, stop=False)
            return r
        s_fold = P.t(t_fold2, deps=allld + [s_ms])

        def v_fold(e):
            e.tensor_tensor(bup2[:, 0:512], pF[0:1, 0:512], upbrow[:, 0:512], ALU.add)
            e.tensor_tensor(bup2[:, 512:1024], pH[0:1, 0:512], upbrow[:, 512:1024], ALU.add)
        s_vfold = P.v(v_fold, deps=[s_fold])

        def v_upscale(e):
            for ci in range(2):
                e.tensor_scalar(stg[:, 2048 + 1024 * ci:2048 + 1024 * (ci + 1)], stg[:, 2048 + 1024 * ci:2048 + 1024 * (ci + 1)], lwc[:, ci:ci + 1], None, ALU.mult)
            e.drain()
            for ci in range(2):
                e.tensor_copy(upbf[:, ci, :], stg[:, 2048 + 1024 * ci:2048 + 1024 * (ci + 1)])
        s_vscale = P.v(v_upscale, deps=[s_vfold])

        def w2_load(e):
            return [
                e.dma_start(out=stg[:, 0:512].rearrange("p (ci n) -> p ci n", n=256), in_=wq.rearrange("(ci p) n -> p ci n", p=128)),
                e.dma_start(out=stg[:, 512:1024].rearrange("p (ci n) -> p ci n", n=256), in_=wk.rearrange("(ci p) n -> p ci n", p=128)),
                e.dma_start(out=stg[:, 1024:1536].rearrange("p (ci n) -> p ci n", n=256), in_=wv.rearrange("(ci p) n -> p ci n", p=128)),
                e.dma_start(out=stg[:, 1536:2048].rearrange("p (ci n) -> p ci n", n=256), in_=wo.rearrange("(ci p) n -> p ci n", p=128)),
                e.dma_start(out=stg[:, 3072:3076].rearrange("p (ci n) -> p ci n", n=2), in_=lrwT.rearrange("(ci p) n -> p ci n", p=128)),
            ]
        s_w2 = P.dma(w2_load, deps=[s_cast1, s_vscale, s_fold], n=5)

        def vcast2(e):
            for ci in range(2):
                e.tensor_copy(wqbf[:, ci, :], stg[:, 256 * ci:256 * (ci + 1)])
                e.tensor_copy(wkbf[:, ci, :], stg[:, 512 + 256 * ci:512 + 256 * (ci + 1)])
                e.tensor_scalar(wvnbf[:, ci, :], stg[:, 1024 + 256 * ci:1024 + 256 * (ci + 1)], -1.0, None, ALU.mult)
                e.tensor_copy(lrwbf[:, ci, :], stg[:, 3072 + 2 * ci:3072 + 2 * (ci + 1)])
        s_cast2 = P.v(vcast2, deps=[s_w2])

        def t_bwo(e):
            clr(e, pG)
            r = None
            for ci in range(2):
                r = e.matmul(pG[0:1, 0:256], pbc[:, ci:ci + 1], stg[:, 1536 + 256 * ci:1536 + 256 * (ci + 1)], start=False, stop=False)
            return r
        s_bwo = P.t(t_bwo, deps=[s_w2, s_ms])
        s_vbwo = P.v(lambda e: e.tensor_copy(bwo[:], pG[0:1, 0:256]), deps=[s_bwo])

        def v_woscale(e):
            for ci in range(2):
                e.tensor_scalar(stg[:, 1536 + 256 * ci:1536 + 256 * (ci + 1)], stg[:, 1536 + 256 * ci:1536 + 256 * (ci + 1)], pwc[:, ci:ci + 1], None, ALU.mult)
            e.drain()
            for ci in range(2):
                e.tensor_copy(wobf[:, ci, :], stg[:, 1536 + 256 * ci:1536 + 256 * (ci + 1)])
        s_vwo = P.v(v_woscale, deps=[s_vbwo])

        def v_misc(e):
            e.tensor_tensor(g2row[:], grow[:], grow[:], ALU.mult)
            e.tensor_tensor(tid[:], r16[:], tokc[:], ALU.add)
            e.tensor_copy(W1f[:, 0, :].rearrange("p (h f) -> p h f", h=2), sW1[:])
            e.tensor_reduce(w1sum[:], sW1[:], AX.X, ALU.add)
            e.tensor_reduce(sb1c[:], b1f[:, 0, :].rearrange("p (h f) -> p h f", h=2), AX.X, ALU.add)
            e.drain()
            e.tensor_copy(W1b[:, 0, :], W1f[:, 0, :])
            e.tensor_copy(w1sumb[:], w1sum[:])
            e.tensor_copy(b1b2[:, 0, 0:256], b1f[:, 0, :])
            e.tensor_copy(b1b2[:, 0, 256:512], b1f[:, 0, :])
        s_vmisc = P.v(v_misc, deps=allld + [s_ms])

        s_relu = P.a(lambda e: e.activation(tid[:], tid[:], AF.Relu, bias=c_zero[0:16, 0:1], scale=1.0), deps=[s_vmisc])

        prevv = [s_cast2, s_vwo]
        wpairs = [("r", None), ("k", None), ("v", None)]
        wdst = {"r": wrtb, "k": wktb, "v": wvtb}
        wsrc = {"r": WrT, "k": WkT, "v": WvT}
        for nm in ("r", "k", "v"):
            for half in range(2):
                s_l = P.dma(lambda e, nm=nm, half=half: e.dma_start(
                    out=stg[:].rearrange("p (ci n) -> p ci n", n=1024),
                    in_=wsrc[nm].rearrange("(ci p) n -> p ci n", p=128)[:, 4 * half:4 * (half + 1), :]), deps=prevv)
                s_c = P.v(lambda e, nm=nm, half=half: [e.tensor_copy(wdst[nm][:, 4 * half + ci, :], stg[:, 1024 * ci:1024 * (ci + 1)]) for ci in range(4)][-1] and None, deps=[s_l])
                prevv = [s_c]
        s_cv = prevv[0]

        idf = sb("idf", [128, 128])
        s_idf = P.v(lambda e: e.tensor_copy(idf[:], sid[:]), deps=allld)
        def t_tidrow(e):
            clr(e, pE)
            return e.matmul(pE[0:1, 0:16], tid[:], idf[0:16, 0:16], start=False, stop=False)
        s_ttr = P.t(t_tidrow, deps=[s_relu, s_idf])
        def v_tidrow(e):
            e.tensor_copy(tidrow[:], pE[0:1, 0:16])
            e.drain()
            for r in range(8):
                e.tensor_copy(tidrow128[:, 16 * r:16 * (r + 1)], tidrow[:])
        s_vtr = P.v(v_tidrow, deps=[s_ttr])

        def t_reps(e):
            clr(e, pF)
            e.matmul(pF[:, 0:256], onesr[0:1, 0:128], grow[:], start=False, stop=False)
            e.matmul(pF[:, 256:512], onesr[0:1, 0:128], g2row[:], start=False, stop=False)
            clr(e, pA)
            e.matmul(pA[:, 0:128], onesr[0:1, 0:128], tidrow128[:], start=False, stop=False)
            e.matmul(pA[:, 128:129], onesr[0:1, 0:128], tidrow[:, 15:16], start=False, stop=False)
            return e.matmul(pA[:, 129:131], onesr[0:1, 0:128], lrbrow[:], start=False, stop=False)
        s_treps = P.t(t_reps, deps=[s_vmisc, s_vtr, s_ms, s_vfold, s_vscale])

        def v_reps(e):
            e.tensor_copy(GREP[:], pF[:, 0:256])
            e.tensor_copy(G2REP[:], pF[:, 256:512])
            e.tensor_copy(t15r[:], pA[:, 128:129])
            e.tensor_copy(lrbB[:], pA[:, 129:131])
            e.drain()
            e.tensor_scalar(nT15d[:], t15r[:], -1.0 / 128, None, ALU.mult)
            e.tensor_scalar(SCALEK[:], smU[:], t15r[:, 0:1], None, ALU.mult)
            e.drain()
            e.tensor_tensor(mtmp[:], smEQ[:], pA[:, 0:128], ALU.mult)
            e.drain()
            e.tensor_tensor(SCALEQ[:], SCALEK[:], mtmp[:], ALU.add)
        s_vreps = P.v(v_reps, deps=[s_treps])

        p0_done = P.barrier()

        st_v3 = {}
        st_store = {}
        p1_prev = {"t1": None, "v2a": None}

        def gen_p1(t):
            bx = t % 2
            depx = [st_v3[t - 2]] if t - 2 in st_v3 else []
            s_lx = P.dma(lambda e, t=t, bx=bx: e.dma_start(out=xbuf[:, bx, :], in_=x_seq[TT * t:TT * (t + 1), :]), deps=p0_done + depx)
            yield

            def v_ln(e, bx=bx):
                xb = xbuf[:, bx, :]
                e.bn_stats(bnb[:, 0:6], xb[:, 0:512])
                e.bn_stats(bnb[:, 6:12], xb[:, 512:1024])
                e.bn_aggr(mv1[:, 0:2], bnb[:, 0:6])
                e.bn_aggr(mv1[:, 2:4], bnb[:, 6:12])
                e.drain()
                e.tensor_tensor(lnt[:, 0:1], mv1[:, 0:1], mv1[:, 2:3], ALU.add)
                e.tensor_tensor(lnt[:, 1:2], mv1[:, 0:1], mv1[:, 2:3], ALU.subtract)
                e.tensor_tensor(lnt[:, 2:3], mv1[:, 1:2], mv1[:, 3:4], ALU.add)
                e.scalar_tensor_tensor(lnt[:, 4:5], lnt[:, 1:2], 0.25, lnt[:, 1:2], ALU.mult, ALU.mult)
                e.tensor_scalar(lnt[:, 3:4], lnt[:, 0:1], 0.5, None, ALU.mult)
                return e.scalar_tensor_tensor(lnt[:, 5:6], lnt[:, 2:3], 0.5, lnt[:, 4:5], ALU.mult, ALU.add)
            dep_ln = [s_lx] + ([p1_prev["t1"]] if p1_prev["t1"] is not None else [])
            s_vln = P.v(v_ln, deps=dep_ln, tag='ln')
            s_astd = P.a(lambda e: e.activation(lnt[:, 6:7], lnt[:, 5:6], AF.Sqrt, bias=c_eps5[:, 0:1], scale=1.0), deps=[s_vln])
            yield

            def v_norm(e, bx=bx):
                e.reciprocal(lnt[:, 7:8], lnt[:, 6:7])
                e.drain()
                return e.tensor_scalar(xln[:, bx, :], xbuf[:, bx, :], lnt[:, 3:4], lnt[:, 7:8], ALU.subtract, ALU.mult)
            s_vnorm = P.v(v_norm, deps=[s_astd], tag='nm')
            yield

            def t_t1(e, bx=bx):
                clr(e, pC)
                r = None
                for ci in range(8):
                    r = e.matmul(pC[:, 64 * ci:64 * (ci + 1)].bitcast(BF16), xln[:, bx, 128 * ci:128 * (ci + 1)], sid[:], is_transpose=True, start=False, stop=False)
                return r
            dep_t1 = [s_vnorm] + ([st_v3[t - 1]] if t - 1 in st_v3 else [])
            s_t1 = P.t(t_t1, deps=dep_t1)
            p1_prev["t1"] = s_t1
            s_a1 = P.a(lambda e: [e.copy(xlnT[:, ci, :], pC[:, 64 * ci:64 * (ci + 1)].bitcast(BF16)) for ci in range(8)], deps=[s_t1], n=8)
            yield

            def t_t2(e):
                clr(e, pD, 512)
                for fo in range(2):
                    for ci in range(8):
                        e.matmul(pD[:, 128 * fo:128 * (fo + 1)], dwbf[:, ci, 128 * fo:128 * (fo + 1)], xlnT[:, ci, :], start=False, stop=False)
                r = None
                for fo in range(2):
                    r = e.matmul(pD[:, 128 * fo:128 * (fo + 1)], dbrow[:, 128 * fo:128 * (fo + 1)], onesr[:, 0:128], start=False, stop=False)
                return r
            dep_t2 = [s_a1] + ([p1_prev["v2a"]] if p1_prev["v2a"] is not None else [])
            s_t2 = P.t(t_t2, deps=dep_t2)
            yield

            def a_silu(e):
                return [e.activation(hT[:, fo, :], pD[:, 128 * fo:128 * (fo + 1)], AF.Silu, bias=c_zero[:, 0:1], scale=1.0) for fo in range(2)]
            s_a2 = P.a(a_silu, deps=[s_t2], n=2)
            yield

            def t_t3b(e):
                clr(e, pD, 520)
                e.matmul(pD[:, 0:256], onesr[0:1, 0:128], brow[:], start=False, stop=False)
                for ci in range(2):
                    e.matmul(pD[:, 0:256], hT[:, ci, :], wqbf[:, ci, :], start=False, stop=False)
                r = None
                for ci in range(2):
                    r = e.matmul(pD[:, 256:258], hT[:, ci, :], lrwbf[:, ci, :], start=False, stop=False)
                return r
            s_t3b = P.t(t_t3b, deps=[s_a2])
            yield

            def a_sig(e):
                return [e.activation(ttlr[:, hh:hh + 1], pD[:, 256 + hh:257 + hh], AF.Sigmoid, bias=lrbB[:, hh:hh + 1], scale=1.0) for hh in range(2)]
            s_a3 = P.a(a_sig, deps=[s_t3b], n=2)
            yield

            def t_t3a(e):
                clr(e, pC)
                e.matmul(pC[:, 0:256], onesr[0:1, 0:128], brow[:], start=False, stop=False)
                for ci in range(2):
                    e.matmul(pC[:, 0:256], hT[:, ci, :], wkbf[:, ci, :], start=False, stop=False)
                    e.matmul(pC[:, 0:256], hT[:, ci, :], wvnbf[:, ci, :], start=False, stop=False)
                r = None
                for ci in range(2):
                    r = e.matmul(pC[:, 256:512], hT[:, ci, :], wkbf[:, ci, :], start=False, stop=False)
                return r
            s_t3a = P.t(t_t3a, deps=[s_a2, s_a1])
            yield

            def v_v2b(e):
                e.tensor_scalar(nglec[:], ttlr[:], nT15d[:, 0:1], None, ALU.mult)
                e.tensor_scalar(nglr[:], ttlr[:], -1.0 / 128, None, ALU.mult)
                e.drain()
                e.tensor_copy(nglecb[:], nglec[:])
                rl = [e.tensor_scalar(lexk[:, 128 * hh:128 * (hh + 1)], pC[:, 256 + 128 * hh:256 + 128 * (hh + 1)], nglec[:, hh:hh + 1], None, ALU.mult) for hh in range(2)]
                rl.append(e.tensor_tensor(gTt[:], pC[:, 0:256], GREP[:], ALU.mult))
                rl.append(e.tensor_copy(xqgb[:], pD[:, 0:256]))
                return rl
            dep_v2b = [s_a3, s_t3a] + ([st_store[t - 1]] if t - 1 in st_store else [])
            s_v2b = P.v(v_v2b, deps=dep_v2b, n=4)
            yield

            def t_t3c(e):
                clr(e, pD)
                r = None
                for hh in range(2):
                    for ci in range(2):
                        e.matmul(pD[:, 128 * hh:128 * (hh + 1)], wkbf[:, ci, 128 * hh:128 * (hh + 1)], hT[:, ci, :], start=False, stop=False)
                        r = e.matmul(pD[:, 256 + 128 * hh:256 + 128 * (hh + 1)], wqbf[:, ci, 128 * hh:128 * (hh + 1)], hT[:, ci, :], start=False, stop=False)
                return r
            s_t3c = P.t(t_t3c, deps=[s_v2b])
            yield

            def v_v2a(e):
                r0 = e.tensor_copy(xkTs[:], pD[:, 0:256])
                r1 = e.tensor_copy(xqTs[:], pD[:, 256:512])
                return [r0, r1]
            dep_v2a = [s_t3c] + ([st_store[t - 1]] if t - 1 in st_store else [])
            s_v2a = P.v(v_v2a, deps=dep_v2a, n=2)
            p1_prev["v2a"] = s_v2a
            yield

            def t_t45(e):
                clr(e, pC)
                r = None
                for hh in range(2):
                    e.matmul(pC[:, 128 * hh:128 * (hh + 1)], xkTs[:, 128 * hh:128 * (hh + 1)], xkTs[:, 128 * hh:128 * (hh + 1)], start=False, stop=False)
                    r = e.matmul(pC[:, 256 + 128 * hh:384 + 128 * hh], xkTs[:, 128 * hh:128 * (hh + 1)], xqTs[:, 128 * hh:128 * (hh + 1)], start=False, stop=False)
                return r
            s_t45 = P.t(t_t45, deps=[s_v2a, s_v2b])
            yield

            def v_v3(e):
                rl = []
                for hh in range(2):
                    e.tensor_scalar(mtmp[:], pC[:, 128 * hh:128 * (hh + 1)], 1.0, nglec[:, hh:hh + 1], ALU.add, ALU.mult)
                    e.tensor_scalar(mtmp2, pC[:, 256 + 128 * hh:384 + 128 * hh], 1.0, nglr[:, hh:hh + 1], ALU.add, ALU.mult)
                    rl.append(e.tensor_tensor(MKt[:, hh, :], mtmp[:], smU[:], ALU.mult))
                    rl.append(e.tensor_tensor(MQt[:, hh, :], mtmp2, SCALEQ[:], ALU.mult))
                return rl
            s_v3 = P.v(v_v3, deps=[s_t45, s_v2b], n=4)
            st_v3[t] = s_v3
            yield

            def s_st(e, t=t):
                return [
                    e.dma_start(out=scrXKT[t], in_=xkTs[:]),
                    e.dma_start(out=scrXQT[t], in_=xqTs[:]),
                    e.dma_start(out=scrLEXK[t], in_=lexk[:]),
                    e.dma_start(out=scrGT[t], in_=gTt[:]),
                    e.dma_start(out=scrMK[t].rearrange("h a b -> a h b"), in_=MKt[:]),
                    e.dma_start(out=scrMQ[t].rearrange("h a b -> a h b"), in_=MQt[:]),
                    e.dma_start(out=scrLE[t], in_=nglecb[:]),
                    e.dma_start(out=scrXQGB[t], in_=xqgb[:]),
                ]
            st_store[t] = P.dma(s_st, deps=[s_v3], n=8)

        st_win = {}
        st_chain = {}
        st_batch = {}
        st_wsub = {}
        st_qdone = {}
        st_ost = {}
        NSC64 = 2 * n_sc
        last_dbg = [None]
        last_dbgq = [None]
        srow = [0, 0, 32, 32]
        scol = [0, 256, 0, 256]

        def w_load(e, t=None, b=None):
            out = [
                e.dma_start(out=wXKT[:, b, :], in_=scrXKT[t]),
                e.dma_start(out=wXQT[:, b, :], in_=scrXQT[t]),
                e.dma_start(out=wLEXK[:, b, :].rearrange("p (s f) -> p s f", s=8),
                            in_=scrLEXK[t].rearrange("(s p) f -> p s f", p=16)),
                e.dma_start(out=wGT[:, b, :].rearrange("p (s f) -> p s f", s=8),
                            in_=scrGT[t].rearrange("(s p) f -> p s f", p=16)),
                e.dma_start(out=wLE[:, b, :].rearrange("p (s h) -> p s h", s=8),
                            in_=scrLE[t].rearrange("(s p) h -> p s h", p=16)),
                e.dma_start(out=wXQGB[:, b, :].rearrange("p (s f) -> p s f", s=8),
                            in_=scrXQGB[t].rearrange("(s p) f -> p s f", p=16)),
            ]
            for qq in range(2):
                for hh in range(2):
                    off = (qq * 2 + hh) * 256
                    out.append(e.dma_start(
                        out=wMK[:, b, off:off + 256].rearrange("p (i j c) -> p i j c", i=4, j=4),
                        in_=scrMK[t, hh][64 * qq:64 * (qq + 1), 64 * qq:64 * (qq + 1)].rearrange("(i r) (j c) -> r i j c", i=4, j=4)))
                    out.append(e.dma_start(
                        out=wMQ[:, b, off:off + 256].rearrange("p (i j c) -> p i j c", i=4, j=4),
                        in_=scrMQ[t, hh][64 * qq:64 * (qq + 1), 64 * qq:64 * (qq + 1)].rearrange("(i r) (j c) -> r i j c", i=4, j=4)))
            return out

        st_p3 = {}
        def gen_p3(t):
            dep_l = [st_ost[(2 * t + 1, 3)]] + ([st_p3[t - 1]] if t - 1 in st_p3 else [])
            s_lo = P.dma(lambda e, t=t: [e.dma_start(out=otile[16 * ss:16 * (ss + 1), :], in_=scrOUTS[t][:, 256 * ss:256 * (ss + 1)]) for ss in range(8)] +
                                        [e.dma_start(out=x3b[:], in_=x_seq[TT * t:TT * (t + 1), :])], deps=dep_l, n=9)
            yield

            def v_ln1(e):
                e.bn_stats(bnb3[:, 0:6], otile[:])
                e.drain()
                return e.bn_aggr(mv3[:, 0:2], bnb3[:, 0:6])
            s_l1 = P.v(v_ln1, deps=[s_lo], tag='ln')
            s_l1s = P.a(lambda e: e.activation(lnt3[:, 6:7], mv3[:, 1:2], AF.Sqrt, bias=c_eps5[:, 0:1], scale=1.0), deps=[s_l1])
            yield

            def v_xh1(e):
                e.reciprocal(lnt3[:, 7:8], lnt3[:, 6:7])
                e.drain()
                return e.tensor_scalar(xhat1[:], otile[:], mv3[:, 0:1], lnt3[:, 7:8], ALU.subtract, ALU.mult)
            s_x1 = P.v(v_xh1, deps=[s_l1s], tag='nm')
            yield

            s_tr1 = P.t(lambda e: [clr(e, pG, 256)] and [e.matmul(pG[:, 64 * ci:64 * (ci + 1)].bitcast(BF16), xhat1[:, 128 * ci:128 * (ci + 1)], sid[:], is_transpose=True, start=False, stop=False) for ci in range(2)][-1], deps=[s_x1])
            s_ev1 = P.a(lambda e: [e.copy(xt1T[:, ci, :], pG[:, 64 * ci:64 * (ci + 1)].bitcast(BF16)) for ci in range(2)], deps=[s_tr1], n=2)
            yield

            def t_womm(e):
                e.matmul(pH[:, 0:256], onesr[0:1, 0:128], bwo[:], start=True, stop=False)
                r = None
                for ci in range(2):
                    r = e.matmul(pH[:, 0:256], xt1T[:, ci, :], wobf[:, ci, :], start=False, stop=False)
                return r
            s_wo = P.t(t_womm, deps=[s_ev1])
            yield

            def v_ln2(e):
                e.bn_stats(bnb3[:, 0:6], pH[:, 0:256])
                e.drain()
                return e.bn_aggr(mv3[:, 0:2], bnb3[:, 0:6])
            s_l2 = P.v(v_ln2, deps=[s_wo], tag='ln')
            s_l2s = P.a(lambda e: e.activation(lnt3[:, 6:7], mv3[:, 1:2], AF.Sqrt, bias=c_eps5[:, 0:1], scale=1.0), deps=[s_l2])
            yield

            def v_xh2(e):
                e.reciprocal(lnt3[:, 7:8], lnt3[:, 6:7])
                e.drain()
                return e.tensor_scalar(xhat2[:], pH[:, 0:256], mv3[:, 0:1], lnt3[:, 7:8], ALU.subtract, ALU.mult)
            s_x2 = P.v(v_xh2, deps=[s_l2s], tag='nm')
            yield

            s_tr2 = P.t(lambda e: [clr(e, pG, 256)] and [e.matmul(pG[:, 64 * ci:64 * (ci + 1)].bitcast(BF16), xhat2[:, 128 * ci:128 * (ci + 1)], sid[:], is_transpose=True, start=False, stop=False) for ci in range(2)][-1], deps=[s_x2])
            s_ev2 = P.a(lambda e: [e.copy(xt1T[:, ci, :], pG[:, 64 * ci:64 * (ci + 1)].bitcast(BF16)) for ci in range(2)], deps=[s_tr2], n=2)
            yield

            def t_up(e):
                e.matmul(pG[:, 0:512], onesr[0:1, 0:128], bup2[:, 0:512], start=True, stop=False)
                e.matmul(pH[:, 0:512], onesr[0:1, 0:128], bup2[:, 512:1024], start=True, stop=False)
                r = None
                for ci in range(2):
                    e.matmul(pG[:, 0:512], xt1T[:, ci, :], upbf[:, ci, 0:512], start=False, stop=False)
                    r = e.matmul(pH[:, 0:512], xt1T[:, ci, :], upbf[:, ci, 512:1024], start=False, stop=False)
                return r
            s_up = P.t(t_up, deps=[s_ev2])
            yield

            def v_ln3(e):
                e.bn_stats(bnb3[:, 0:6], pG[:, 0:512])
                e.bn_stats(bnb3[:, 6:12], pH[:, 0:512])
                e.bn_aggr(mv3[:, 0:2], bnb3[:, 0:6])
                e.bn_aggr(mv3[:, 2:4], bnb3[:, 6:12])
                e.drain()
                e.tensor_tensor(lnt3[:, 0:1], mv3[:, 0:1], mv3[:, 2:3], ALU.add)
                e.tensor_tensor(lnt3[:, 1:2], mv3[:, 0:1], mv3[:, 2:3], ALU.subtract)
                e.tensor_tensor(lnt3[:, 2:3], mv3[:, 1:2], mv3[:, 3:4], ALU.add)
                e.scalar_tensor_tensor(lnt3[:, 4:5], lnt3[:, 1:2], 0.25, lnt3[:, 1:2], ALU.mult, ALU.mult)
                e.tensor_scalar(lnt3[:, 3:4], lnt3[:, 0:1], 0.5, None, ALU.mult)
                return e.scalar_tensor_tensor(lnt3[:, 5:6], lnt3[:, 2:3], 0.5, lnt3[:, 4:5], ALU.mult, ALU.add)
            s_l3 = P.v(v_ln3, deps=[s_up], tag='ln')
            s_l3s = P.a(lambda e: e.activation(lnt3[:, 6:7], lnt3[:, 5:6], AF.Sqrt, bias=c_eps5[:, 0:1], scale=1.0), deps=[s_l3])
            yield

            def v_x2t(e):
                e.reciprocal(lnt3[:, 7:8], lnt3[:, 6:7])
                e.drain()
                e.tensor_scalar(xh3[:, 0:512], pG[:, 0:512], lnt3[:, 3:4], lnt3[:, 7:8], ALU.subtract, ALU.mult)
                e.tensor_scalar(xh3[:, 512:1024], pH[:, 0:512], lnt3[:, 3:4], lnt3[:, 7:8], ALU.subtract, ALU.mult)
                e.drain()
                return e.tensor_tensor(x2t, xh3, x3b[:], ALU.add)
            s_vx2 = P.v(v_x2t, deps=[s_l3s], tag='nm')
            s_sx2 = P.dma(lambda e, t=t: e.dma_start(out=scrX2[t], in_=x2t), deps=[s_vx2])
            st_p3[t] = s_sx2

        st_p4 = {}
        def gen_p4(t):
            dep_l = [st_p3[t]] + ([st_p4[t - 1]] if t - 1 in st_p4 else [])
            s_lo = P.dma(lambda e, t=t: e.dma_start(out=x24[:], in_=scrX2[t]), deps=dep_l)
            yield

            def v_ln4(e):
                e.bn_stats(bnb4[:, 0:6], x24[:, 0:512])
                e.bn_stats(bnb4[:, 6:12], x24[:, 512:1024])
                e.bn_aggr(mv4[:, 0:2], bnb4[:, 0:6])
                e.bn_aggr(mv4[:, 2:4], bnb4[:, 6:12])
                e.drain()
                e.tensor_tensor(lnt4[:, 0:1], mv4[:, 0:1], mv4[:, 2:3], ALU.add)
                e.tensor_tensor(lnt4[:, 1:2], mv4[:, 0:1], mv4[:, 2:3], ALU.subtract)
                e.tensor_tensor(lnt4[:, 2:3], mv4[:, 1:2], mv4[:, 3:4], ALU.add)
                e.scalar_tensor_tensor(lnt4[:, 4:5], lnt4[:, 1:2], 0.25, lnt4[:, 1:2], ALU.mult, ALU.mult)
                e.tensor_scalar(lnt4[:, 3:4], lnt4[:, 0:1], 0.5, None, ALU.mult)
                return e.scalar_tensor_tensor(lnt4[:, 5:6], lnt4[:, 2:3], 0.5, lnt4[:, 4:5], ALU.mult, ALU.add)
            s_l4 = P.v(v_ln4, deps=[s_lo], tag='ln')
            s_l4s = P.a(lambda e: e.activation(lnt4[:, 6:7], lnt4[:, 5:6], AF.Sqrt, bias=c_eps5[:, 0:1], scale=1.0), deps=[s_l4])
            yield

            def v_n4(e):
                e.reciprocal(lnt4[:, 7:8], lnt4[:, 6:7])
                e.drain()
                return e.tensor_scalar(xln4[:], x24[:], lnt4[:, 3:4], lnt4[:, 7:8], ALU.subtract, ALU.mult)
            s_n4 = P.v(v_n4, deps=[s_l4s], tag='nm')
            yield

            s_tr4 = P.t(lambda e: [clr(e, pG)] and [e.matmul(pG[:, 64 * ci:64 * (ci + 1)].bitcast(BF16), xln4[:, 128 * ci:128 * (ci + 1)], sid[:], is_transpose=True, start=False, stop=False) for ci in range(8)][-1], deps=[s_n4])
            s_ev4 = P.a(lambda e: [e.copy(xlnT4[:, ci, :], pG[:, 64 * ci:64 * (ci + 1)].bitcast(BF16)) for ci in range(8)], deps=[s_tr4], n=8)
            yield

            def v_mix(e, t=t):
                if t == 0:
                    e.memset(prevc[:], 0.0)
                e.tensor_copy(xprevT[:, :, 1:128], xlnT4[:, :, 0:127])
                e.tensor_copy(xprevT[:, :, 0:1], prevc[:].rearrange("p (a b) -> p a b", b=1))
                e.drain()
                e.tensor_copy(prevc[:], xlnT4[:, :, 127:128].rearrange("p a b -> p (a b)"))
                e.tensor_tensor(dTt[:], xlnT4[:].rearrange("p a b -> p (a b)"), xprevT[:].rearrange("p a b -> p (a b)"), ALU.subtract)
                e.drain()
                for ci in range(8):
                    e.tensor_scalar(xrT[:, 128 * ci:128 * (ci + 1)], dTt[:, 128 * ci:128 * (ci + 1)], murc[:, ci:ci + 1], None, ALU.mult)
                    e.tensor_scalar(xkmT[:, ci, :], dTt[:, 128 * ci:128 * (ci + 1)], mukc[:, ci:ci + 1], None, ALU.mult)
                e.drain()
                e.tensor_tensor(xkmT[:], xkmT[:], xprevT[:], ALU.add)
                return e.tensor_tensor(xrT[:], xrT[:], xprevT[:].rearrange("p a b -> p (a b)"), ALU.add)
            s_mx = P.v(v_mix, deps=[s_ev4], tag='mx')
            yield

            def t_rk(e):
                r = None
                for ci in range(8):
                    e.matmul(pG[:, 0:512], xrT[:, 128 * ci:128 * (ci + 1)], wrtb[:, ci, 0:512], start=(ci == 0), stop=False)
                    r = e.matmul(pH[:, 0:512], xrT[:, 128 * ci:128 * (ci + 1)], wrtb[:, ci, 512:1024], start=(ci == 0), stop=False)
                return r
            s_rk = P.t(t_rk, deps=[s_mx])
            yield

            def a_r(e):
                r0 = e.activation(rsg[:, 0:512], pG[:, 0:512], AF.Sigmoid, bias=c_zero[:, 0:1], scale=1.0)
                r1 = e.activation(rsg[:, 512:1024], pH[:, 0:512], AF.Sigmoid, bias=c_zero[:, 0:1], scale=1.0)
                return [r0, r1]
            s_ar = P.a(a_r, deps=[s_rk], n=2)
            yield

            def t_k(e):
                r = None
                for ci in range(8):
                    e.matmul(pG[:, 0:512], xkmT[:, ci, :], wktb[:, ci, 0:512], start=(ci == 0), stop=False)
                    r = e.matmul(pH[:, 0:512], xkmT[:, ci, :], wktb[:, ci, 512:1024], start=(ci == 0), stop=False)
                return r
            s_tk = P.t(t_k, deps=[s_ar])
            yield

            def a_k(e):
                e.activation(kact[:, 0:512], pG[:, 0:512], AF.Relu, bias=c_zero[:, 0:1], scale=1.0)
                e.activation(kact[:, 512:1024], pH[:, 0:512], AF.Relu, bias=c_zero[:, 0:1], scale=1.0)
                r0 = e.activation(kact[:, 0:512], kact[:, 0:512], AF.Square, bias=c_zero[:, 0:1], scale=1.0)
                r1 = e.activation(kact[:, 512:1024], kact[:, 512:1024], AF.Square, bias=c_zero[:, 0:1], scale=1.0)
                return [r0, r1]
            s_ark = P.a(a_k, deps=[s_tk], n=2)
            yield

            s_tr5 = P.t(lambda e: [clr(e, pG)] and [e.matmul(pG[:, 64 * ci:64 * (ci + 1)].bitcast(BF16), kact[:, 128 * ci:128 * (ci + 1)], sid[:], is_transpose=True, start=False, stop=False) for ci in range(8)][-1], deps=[s_ark])
            s_ev5 = P.a(lambda e: [e.copy(kactT[:, ci, :], pG[:, 64 * ci:64 * (ci + 1)].bitcast(BF16)) for ci in range(8)], deps=[s_tr5], n=8)
            yield

            def t_v(e):
                r = None
                for ci in range(8):
                    e.matmul(pG[:, 0:512], kactT[:, ci, :], wvtb[:, ci, 0:512], start=(ci == 0), stop=False)
                    r = e.matmul(pH[:, 0:512], kactT[:, ci, :], wvtb[:, ci, 512:1024], start=(ci == 0), stop=False)
                return r
            s_tv = P.t(t_v, deps=[s_ev5])
            yield

            def v_ffo(e):
                e.tensor_tensor(ffo[:, 0:512], pG[:, 0:512], rsg[:, 0:512], ALU.mult)
                e.tensor_tensor(ffo[:, 512:1024], pH[:, 0:512], rsg[:, 512:1024], ALU.mult)
                e.bn_stats(bnb4[:, 0:6], ffo[:, 0:512])
                e.bn_stats(bnb4[:, 6:12], ffo[:, 512:1024])
                e.bn_aggr(mv4[:, 0:2], bnb4[:, 0:6])
                e.bn_aggr(mv4[:, 2:4], bnb4[:, 6:12])
                e.drain()
                e.tensor_tensor(lnt4[:, 0:1], mv4[:, 0:1], mv4[:, 2:3], ALU.add)
                e.tensor_tensor(lnt4[:, 1:2], mv4[:, 0:1], mv4[:, 2:3], ALU.subtract)
                e.tensor_tensor(lnt4[:, 2:3], mv4[:, 1:2], mv4[:, 3:4], ALU.add)
                e.scalar_tensor_tensor(lnt4[:, 4:5], lnt4[:, 1:2], 0.25, lnt4[:, 1:2], ALU.mult, ALU.mult)
                e.tensor_scalar(lnt4[:, 3:4], lnt4[:, 0:1], 0.5, None, ALU.mult)
                return e.scalar_tensor_tensor(lnt4[:, 5:6], lnt4[:, 2:3], 0.5, lnt4[:, 4:5], ALU.mult, ALU.add)
            s_ff = P.v(v_ffo, deps=[s_tv, s_ark], tag='ln')
            s_ffs = P.a(lambda e: e.activation(lnt4[:, 6:7], lnt4[:, 5:6], AF.Sqrt, bias=c_eps5[:, 0:1], scale=1.0), deps=[s_ff])
            yield

            def v_fin(e):
                e.reciprocal(lnt4[:, 7:8], lnt4[:, 6:7])
                e.drain()
                e.tensor_scalar(xh4[:, 0:512], ffo[:, 0:512], lnt4[:, 3:4], lnt4[:, 7:8], ALU.subtract, ALU.mult)
                e.tensor_scalar(xh4[:, 512:1024], ffo[:, 512:1024], lnt4[:, 3:4], lnt4[:, 7:8], ALU.subtract, ALU.mult)
                e.drain()
                return e.tensor_tensor(fout, xh4, x24[:], ALU.add)
            s_vf = P.v(v_fin, deps=[s_ffs], tag='nm')
            s_y = P.dma(lambda e, t=t: e.dma_start(out=y[TT * t:TT * (t + 1), :], in_=fout), deps=[s_vf])
            st_p4[t] = s_y


        workq = []

        def pump(k):
            n = 0
            while workq and n < k:
                try:
                    next(workq[0])
                    n += 1
                except StopIteration:
                    workq.pop(0)

        def drain_q():
            while workq:
                pump(1 << 20)

        for t0 in range(min(3, n_sc)):
            workq.append(gen_p1(t0))
        drain_q()

        for sc in range(NSC64):
            t128, qh = sc // 2, sc % 2
            b = t128 % 2
            par, nxt = sc % 2, (sc + 1) % 2
            if qh == 0:
                while t128 not in st_store:
                    pump(64)
                deps_w = [st_store[t128]]
                if t128 - 2 >= 0:
                    deps_w += [st_chain[(2 * t128 - 3, 3)], st_batch[(2 * t128 - 3, 3)], st_qdone[(2 * t128 - 3, 3)]]
                st_win[t128] = P.dma(lambda e, t=t128, b=b: w_load(e, t, b), deps=deps_w, n=14)
            s_w = st_win[t128]

            deps_blk = [s_w]
            if sc > 0:
                deps_blk += [st_wsub[sc - 1], st_chain[(sc - 1, 3)], st_qdone[(sc - 1, 3)]]

            def t_blk(e, sc=sc, qh=qh, b=b, par=par):
                for bank in (pA, pB):
                    e.matmul(bank[0:48, 0:512], onesb[0:1, 0:48], b1b2[:, par, :], start=True, stop=False)
                r = None
                for j in range(4):
                    ss = 4 * qh + j
                    ro, co = srow[j], scol[j]
                    for hh in range(2):
                        e.matmul(pA[ro:ro + 16, co + 128 * hh:co + 128 * (hh + 1)],
                                 wXKT[:, b, 128 * hh + 16 * ss:128 * hh + 16 * (ss + 1)],
                                 W1b[:, par, 128 * hh:128 * (hh + 1)], start=False, stop=False)
                        r = e.matmul(pB[ro:ro + 16, co + 128 * hh:co + 128 * (hh + 1)],
                                 wXQT[:, b, 128 * hh + 16 * ss:128 * hh + 16 * (ss + 1)],
                                 W1b[:, par, 128 * hh:128 * (hh + 1)], start=False, stop=False)
                clr(e, pE, 512)
                r = clr(e, pF, 512)
                return r
            s_blk = P.t(t_blk, deps=deps_blk)
            st_batch[(sc, -1)] = s_blk

            for j in range(4):
                ss = 4 * qh + j
                ro, co = srow[j], scol[j]
                dep_c = [st_batch[(sc, j - 1)]]
                if dbg and last_dbg[0] is not None:
                    dep_c.append(last_dbg[0])

                def v_chain(e, ro=ro, co=co):
                    e.tensor_copy(sxcsq[:, 0:128], pA[ro:ro + 16, co:co + 128])
                    e.tensor_copy(sxcsq[:, 128:256], pA[ro:ro + 16, co + 128:co + 256])
                    e.tensor_tensor(sxcsq[:, 256:384], sxcsq[:, 0:128], sxcsq[:, 0:128], ALU.mult)
                    e.tensor_tensor(sxcsq[:, 384:512], sxcsq[:, 128:256], sxcsq[:, 128:256], ALU.mult)
                    e.tensor_reduce(sms[:, 0:2], sxcsq[:, 0:256].rearrange("p (g f) -> p g f", g=2), AX.X, ALU.add)
                    e.tensor_reduce(sms[:, 2:4], sxcsq[:, 256:512].rearrange("p (g f) -> p g f", g=2), AX.X, ALU.add)
                    e.tensor_scalar(sms[:, 0:2], sms[:, 0:2], 1.0 / 128, None, ALU.mult)
                    e.tensor_scalar(sms[:, 2:4], sms[:, 2:4], 1.0 / 128, None, ALU.mult)
                    e.tensor_tensor(smsq[:], sms[:, 0:2], sms[:, 0:2], ALU.mult)
                    e.tensor_copy(svf[:], sms[:, 0:1])
                    return e.tensor_tensor(svar[:, 0:2], sms[:, 2:4], smsq[:], ALU.subtract)
                s_var = P.v(v_chain, deps=dep_c, tag='chain1')
                dep_std = [s_var]
                if dbg:
                    s_dbg = P.dma(lambda e, sc=sc, j=j: e.dma_start(out=scrDBG[sc, j], in_=sxc), deps=[s_var])
                    last_dbg[0] = s_dbg
                    dep_std = [s_var, s_dbg]
                s_std = P.a(lambda e: e.activation(sstd[:], svar[:], AF.Sqrt, bias=c_eps6[0:16, 0:1], scale=1.0), deps=dep_std)

                def v_chain2(e, b=b, ss=ss):
                    e.reciprocal(sinv[:], sstd[:])
                    e.tensor_scalar(sxh[:, 0:128], sxcsq[:, 0:128], sms[:, 0:1], None, ALU.subtract)
                    e.tensor_scalar(sxh[:, 128:256], sxcsq[:, 128:256], sms[:, 1:2], None, ALU.subtract)
                    e.tensor_scalar(sxh[:, 0:128], sxh[:, 0:128], sinv[:, 0:1], None, ALU.mult)
                    e.tensor_scalar(sxh[:, 128:256], sxh[:, 128:256], sinv[:, 1:2], None, ALU.mult)
                    e.tensor_tensor(sgyp[:, 0:128], sxh[:, 0:128], G2REP[0:16, 0:128], ALU.mult)
                    e.tensor_tensor(sgyp[:, 128:256], sxh[:, 128:256], G2REP[0:16, 128:256], ALU.mult)
                    e.tensor_tensor(sgp[:, 0:128], sgyp[:, 0:128], wGT[:, b, 256 * ss:256 * ss + 128], ALU.add)
                    e.tensor_tensor(sgp[:, 128:256], sgyp[:, 128:256], wGT[:, b, 256 * ss + 128:256 * ss + 256], ALU.add)
                    e.tensor_tensor(sgp[:, 256:384], sgp[:, 0:128], sxh[:, 0:128], ALU.mult)
                    e.tensor_tensor(sgp[:, 384:512], sgp[:, 128:256], sxh[:, 128:256], ALU.mult)
                    e.tensor_reduce(ss12[:, 0:2], sgp[:, 0:256].rearrange("p (g f) -> p g f", g=2), AX.X, ALU.add)
                    e.tensor_reduce(ss12[:, 2:4], sgp[:, 256:512].rearrange("p (g f) -> p g f", g=2), AX.X, ALU.add)
                    e.tensor_scalar(sninv[:], sinv[:], -1.0, None, ALU.mult)
                    e.tensor_scalar(sm12[:, 2:4], ss12[:, 2:4], 1.0 / 128, None, ALU.mult)
                    e.tensor_scalar(sm12[:, 0:2], ss12[:, 0:2], 1.0 / 128, None, ALU.mult)
                    e.scalar_tensor_tensor(ssu[:, 0:128], sxh[:, 0:128], sm12[:, 2:3], sgp[:, 0:128], ALU.mult, ALU.subtract)
                    e.scalar_tensor_tensor(ssu[:, 128:256], sxh[:, 128:256], sm12[:, 3:4], sgp[:, 128:256], ALU.mult, ALU.subtract)
                    g0 = e.tensor_scalar(gradb[:, 0:128], ssu[:, 0:128], sm12[:, 0:1], sninv[:, 0:1], ALU.add, ALU.mult)
                    g1 = e.tensor_scalar(gradb[:, 128:256], ssu[:, 128:256], sm12[:, 1:2], sninv[:, 1:2], ALU.add, ALU.mult)
                    return [g0, g1]
                s_grad = P.v(v_chain2, deps=[s_std], n=2)
                st_chain[(sc, j)] = s_grad

                def t_batch(e, sc=sc, qh=qh, j=j, b=b):
                    r = None
                    for hh in range(2):
                        gslice = gradb[:, 128 * hh:128 * (hh + 1)]
                        moff = (qh * 2 + hh) * 256 + j * 64
                        for j2 in range(j + 1, 4):
                            r = e.matmul(pA[srow[j2]:srow[j2] + 16, scol[j2] + 128 * hh:scol[j2] + 128 * (hh + 1)],
                                         wMK[:, b, moff + 16 * j2:moff + 16 * (j2 + 1)],
                                         gslice, start=False, stop=False)
                        for j2 in range(j, 4):
                            r = e.matmul(pB[srow[j2]:srow[j2] + 16, scol[j2] + 128 * hh:scol[j2] + 128 * (hh + 1)],
                                         wMQ[:, b, moff + 16 * j2:moff + 16 * (j2 + 1)],
                                         gslice, start=False, stop=False)
                        bankW = pE if hh == 0 else pF
                        ss = 4 * qh + j
                        r = e.matmul(bankW[:, 0:128], wLEXK[:, b, 256 * ss + 128 * hh:256 * ss + 128 * (hh + 1)], gslice, start=False, stop=False)
                        r = e.matmul(bankW[0:1, 128:256], wLE[:, b, 2 * ss + hh:2 * ss + hh + 1], gslice, start=False, stop=False)
                    return r
                s_b = P.t(t_batch, deps=[s_grad])
                st_batch[(sc, j)] = s_b

                def a_qevac(e, ro=ro, co=co):
                    r0 = e.copy(qxc, pB[ro:ro + 16, co:co + 256])
                    r1 = e.activation(qsq, pB[ro:ro + 16, co:co + 256], AF.Square, bias=c_zero[0:16, 0:1], scale=1.0)
                    return [r0, r1]
                dep_q = [s_b] + ([last_dbgq[0]] if dbg and last_dbgq[0] is not None else [])
                if (sc, j - 1) in st_qdone:
                    dep_q.append(st_qdone[(sc, j - 1)])
                elif sc > 0:
                    dep_q.append(st_qdone[(sc - 1, 3)])
                s_qe = P.a(a_qevac, deps=dep_q, n=2)

                if dbg:
                    s_dbgq = P.dma(lambda e, sc=sc, j=j: e.dma_start(out=scrDBGQ[sc, j], in_=qxc), deps=[s_qe])
                    last_dbgq[0] = s_dbgq
                def v_qv(e):
                    return e.tensor_reduce(qms[:], qxcsq[:].rearrange("p (g f) -> p g f", g=4), AX.X, ALU.add)
                s_qv = P.v(v_qv, deps=[s_qe], tag='chain2')

                def g_qm(e):
                    e.tensor_scalar(qms[:, 0:2], qms[:, 0:2], 1.0 / 128, None, ALU.mult)
                    e.tensor_scalar(qms[:, 2:4], qms[:, 2:4], 1.0 / 128, None, ALU.mult)
                    e.tensor_tensor(qmsq[:], qms[:, 0:2], qms[:, 0:2], ALU.mult)
                    e.tensor_copy(qvf[:], qms[:, 0:1])
                    return e.tensor_tensor(qvar[:, 0:2], qms[:, 2:4], qmsq[:], ALU.subtract)
                s_qm = P.g(g_qm, deps=[s_qv])
                s_qs = P.a(lambda e: e.activation(qstd[:], qvar[:], AF.Sqrt, bias=c_eps6[0:16, 0:1], scale=1.0), deps=[s_qm])

                def v_qt(e):
                    e.reciprocal(qinv[:], qstd[:])
                dep_qt = [s_qs]
                if (sc, j - 1) in st_qdone:
                    dep_qt.append(st_qdone[(sc, j - 1)])
                elif sc > 0:
                    dep_qt.append(st_qdone[(sc - 1, 3)])
                s_qt = P.v(v_qt, deps=dep_qt)

                def g_out(e, b=b, ss=ss, j=j):
                    e.tensor_scalar(qt1[:, 0:128], qxc[:, 0:128], qms[:, 0:1], qinv[:, 0:1], ALU.subtract, ALU.mult)
                    e.tensor_scalar(qt1[:, 128:256], qxc[:, 128:256], qms[:, 1:2], qinv[:, 1:2], ALU.subtract, ALU.mult)
                    e.tensor_tensor(qt2[:, 0:128], qt1[:, 0:128], GREP[0:16, 0:128], ALU.mult)
                    e.tensor_tensor(qt2[:, 128:256], qt1[:, 128:256], GREP[0:16, 128:256], ALU.mult)
                    e.tensor_copy(qvf[:], qms[:, 0:1])
                    return e.tensor_tensor(outsb[:, j % 2, :], qt2[:], wXQGB[:, b, 256 * ss:256 * (ss + 1)], ALU.add)
                dep_go = [s_qt, s_w]
                key_prev = (sc, j - 2) if j >= 2 else ((sc - 1, j + 2) if sc > 0 else None)
                if key_prev is not None and key_prev in st_ost:
                    dep_go.append(st_ost[key_prev])
                s_g = P.g(g_out, deps=dep_go)
                st_qdone[(sc, j)] = s_g
                s_os = P.dma(lambda e, t=t128, ss=ss, j=j: e.dma_start(out=scrOUTS[t][:, 256 * ss:256 * (ss + 1)], in_=outsb[:, j % 2, :]), deps=[s_g])
                st_ost[(sc, j)] = s_os
                pump(7)

            def v_wsub(e, par=par, nxt=nxt):
                for hh in range(2):
                    bankW = pE if hh == 0 else pF
                    e.tensor_tensor(W1f[:, nxt, 128 * hh:128 * (hh + 1)], W1f[:, par, 128 * hh:128 * (hh + 1)], bankW[:, 0:128], ALU.add)
                    e.tensor_tensor(b1f[:, nxt, 128 * hh:128 * (hh + 1)], b1f[:, par, 128 * hh:128 * (hh + 1)], bankW[0:1, 128:256], ALU.add)
                r0 = e.tensor_copy(W1b[:, nxt, :], W1f[:, nxt, :])
                r1 = e.tensor_copy(b1b2[:, nxt, 0:256], b1f[:, nxt, :])
                r2 = e.tensor_copy(b1b2[:, nxt, 256:512], b1f[:, nxt, :])
                return [r0, r1, r2]
            s_ws = P.v(v_wsub, deps=[st_batch[(sc, 3)]], n=3)
            st_wsub[sc] = s_ws
            if qh == 1:
                if t128 + 3 < n_sc:
                    workq.append(gen_p1(t128 + 3))
                if t128 - 1 >= 0:
                    workq.append(gen_p3(t128 - 1))
                if t128 - 2 >= 0:
                    workq.append(gen_p4(t128 - 2))

        drain_q()
        workq.append(gen_p3(n_sc - 1))
        if n_sc - 2 >= 0:
            workq.append(gen_p4(n_sc - 2))
        workq.append(gen_p4(n_sc - 1))
        drain_q()
        P.g(lambda e: None, deps=[st_p4[n_sc - 1]])
        P.t(lambda e: None, deps=[st_p4[n_sc - 1]])

        if max_steps is not None:
            P.steps = P.steps[:max_steps]
        block = es.enter_context(nc.Block())
        P.finalize(block, es)
    return nc, inp


def prep_inputs(xs, w):
    import ml_dtypes
    L = xs.shape[0]
    f32 = np.float32
    r = {}
    r["x_seq"] = np.ascontiguousarray(xs, f32)
    r["down_w"] = np.ascontiguousarray(w["down_w"], f32)
    r["down_b"] = np.ascontiguousarray(w["down_b"].reshape(1, 256), f32)
    r["up_w"] = np.ascontiguousarray(w["up_w"], f32)
    r["up_b"] = np.ascontiguousarray(w["up_b"].reshape(1, 1024), f32)
    for k in ("wq", "wk", "wv", "wo"):
        r[k] = np.ascontiguousarray(w[k], f32)
    r["W1in"] = np.ascontiguousarray(w["W1"], f32)
    r["b1in"] = np.ascontiguousarray(w["b1"].reshape(1, 256), f32)
    r["ttt_g"] = np.ascontiguousarray(w["ttt_ln_w"].reshape(1, 256), f32)
    r["ttt_b"] = np.ascontiguousarray(w["ttt_ln_b"].reshape(1, 256), f32)
    r["lrwT"] = np.ascontiguousarray(w["lr_w"].reshape(2, 256).T, f32)
    r["lr_b"] = np.ascontiguousarray(w["lr_b"].reshape(1, 2), f32)
    r["tok_idx"] = np.ascontiguousarray(w["tok_idx"].reshape(16, 1), f32)
    r["recip16"] = np.ascontiguousarray((1.0 / np.arange(1, 17)).reshape(16, 1), f32)
    r["post_w"] = np.ascontiguousarray(w["post_w"].reshape(1, 256), f32)
    r["post_b"] = np.ascontiguousarray(w["post_b"].reshape(1, 256), f32)
    r["ln_w"] = np.ascontiguousarray(w["ln_w"].reshape(1, 256), f32)
    r["ln_b"] = np.ascontiguousarray(w["ln_b"].reshape(1, 256), f32)
    r["postw_c"] = np.ascontiguousarray(w["post_w"].reshape(2, 128).T, f32)
    r["postb_c"] = np.ascontiguousarray(w["post_b"].reshape(2, 128).T, f32)
    r["lnw_c"] = np.ascontiguousarray(w["ln_w"].reshape(2, 128).T, f32)
    r["lnb_c"] = np.ascontiguousarray(w["ln_b"].reshape(2, 128).T, f32)
    r["WrT"] = np.ascontiguousarray(w["W_r"].T, f32)
    r["WkT"] = np.ascontiguousarray(w["W_k"].T, f32)
    r["WvT"] = np.ascontiguousarray(w["W_v"].T, f32)
    r["mur_c"] = np.ascontiguousarray(w["mu_r"].reshape(8, 128).T, f32)
    r["muk_c"] = np.ascontiguousarray(w["mu_k"].reshape(8, 128).T, f32)
    r["identb"] = np.eye(128, dtype=ml_dtypes.bfloat16)
    ri, ci = np.mgrid[0:128, 0:128]
    r["maskU"] = ((ci // 16) > (ri // 16)).astype(f32)
    r["maskEQT"] = (((ci // 16) == (ri // 16)) & ((ci % 16) >= (ri % 16))).astype(f32)
    return r


N_SC = 64

_BUILT = {}

def _get_nc():
    if "nc" not in _BUILT:
        _BUILT["nc"] = build(N_SC)[0]
    return _BUILT["nc"]


def kernel(**inputs):
    from concourse.bass_utils import run_bass_kernel_spmd
    w = {
        "down_w": inputs["L0_attn_down_w"], "down_b": inputs["L0_attn_down_b"],
        "up_w": inputs["L0_attn_up_w"], "up_b": inputs["L0_attn_up_b"],
        "wq": inputs["L0_attn_wq"], "wk": inputs["L0_attn_wk"],
        "wv": inputs["L0_attn_wv"], "wo": inputs["L0_attn_wo"],
        "W1": inputs["L0_attn_W1"], "b1": inputs["L0_attn_b1"],
        "ttt_ln_w": inputs["L0_attn_ttt_ln_w"], "ttt_ln_b": inputs["L0_attn_ttt_ln_b"],
        "lr_w": inputs["L0_attn_lr_w"], "lr_b": inputs["L0_attn_lr_b"],
        "tok_idx": inputs["L0_attn_tok_idx"],
        "post_w": inputs["L0_attn_post_w"], "post_b": inputs["L0_attn_post_b"],
        "ln_w": inputs["L0_attn_ln_w"], "ln_b": inputs["L0_attn_ln_b"],
        "W_r": inputs["L0_ffn_W_r"], "W_k": inputs["L0_ffn_W_k"], "W_v": inputs["L0_ffn_W_v"],
        "mu_r": inputs["L0_ffn_mu_r"], "mu_k": inputs["L0_ffn_mu_k"],
    }
    w = {k: np.asarray(v) for k, v in w.items()}
    x = np.asarray(inputs["L0_attn_x"])
    nc = _get_nc()
    im0 = prep_inputs(np.ascontiguousarray(x[0], np.float32), w)
    im1 = prep_inputs(np.ascontiguousarray(x[1], np.float32), w)
    in_maps = [im0, im0, im0, im0, im1, im1, im1, im1]
    res = run_bass_kernel_spmd(nc, in_maps, core_ids=list(range(8)))
    y0 = np.asarray(res.results[0]["y"], np.float32)
    y1 = np.asarray(res.results[4]["y"], np.float32)
    return np.stack([y0, y1], axis=0)
